# version 01
# baseline (speedup 1.0000x reference)
import numpy as np
import ml_dtypes
from contextlib import ExitStack

import concourse.bass as bass
import concourse.mybir as mybir
from concourse.bass_utils import run_bass_kernel_spmd

F32 = mybir.dt.float32
BF16 = mybir.dt.bfloat16
AF = mybir.ActivationFunctionType
ALU = mybir.AluOpType
AX = mybir.AxisListType

D, DI, DS, NE, HID, T = 1024, 2048, 16, 6, 2048, 2048
NTILE = T // 128
EPS = 1e-5
TC = 512
NCH = T // TC
TC2 = 256
NCH2 = T // TC2

DEBUG_TAPS = ()
ENGS = ("sync", "gpsimd", "vector", "scalar", "tensor")


class Planner:
    def __init__(self, nc, block, sems):
        self.nc, self.block, self.sems = nc, block, sems
        self.cnt = {k: 0 for k in sems}
        self.q = {e: [] for e in ENGS}

    def do(self, eng, fn, inc=None):
        if inc is None:
            self.q[eng].append(fn)
            return None
        name, k = inc
        h = self.sems[name]
        self.cnt[name] += k
        tok = self.cnt[name]

        def wrapped(e, fn=fn, h=h, k=k):
            fn(e).then_inc(h, k)
        self.q[eng].append(wrapped)
        return (name, tok)

    def wait(self, eng, tok):
        if not tok:
            return
        name, v = tok
        if v <= 0:
            return
        h = self.sems[name]
        self.q[eng].append(lambda e, h=h, v=v: e.wait_ge(h, v))

    def flush(self):
        for eng in ENGS:
            items = self.q[eng]
            if not items:
                continue
            self.q[eng] = []

            def body(e, items=items):
                for it in items:
                    it(e)
            getattr(self.block, eng)(body)


def build_nc(debug_taps=(), races=False):
    nc = bass.Bass("TRN2", target_bir_lowering=False,
                   detect_race_conditions=races)

    def din(name, shape, dt=F32):
        return nc.dram_tensor(name, shape, dt, kind="ExternalInput").ap()

    x_h = din("x", [T, D])
    xhalo_h = din("xhalo", [128, D])
    hmask_h = din("hmask", [128, 1])
    cmask_h = din("cmask", [128, 8])
    ipw_h = din("ipw", [D, 2 * DI])
    ipb_h = din("ipb", [128, 32])
    cw_h = din("cw", [128, 16, 4])
    cb_h = din("cb", [128, 16])
    xpwd_h = din("xpwd", [DI, 32])
    xpwu_h = din("xpwu", [DI, DI])
    xpbd_h = din("xpbd", [32, 1])
    xpbu_h = din("xpbu", [128, 16])
    dtw_h = din("dtw", [DS, DI])
    dtb_h = din("dtb", [128, 16])
    dvec_h = din("dvec", [128, 16])
    ow_h = din("ow", [DI, D])
    ob_h = din("ob", [1, D])
    wg_h = din("wg", [D, NE])
    bg_h = din("bg", [NE, 1])
    w1_h = din("w1", [NE, D, HID])
    b1_h = din("b1", [128, NE, HID // 128])
    w2_h = din("w2", [NE, HID, D])
    b2_h = din("b2", [NE, D])
    ident_h = din("ident", [128, 128])
    ones1_h = din("ones1", [1, 128])
    selb_h = din("selb", [32, DS * 128])
    signv_h = din("signv", [32, 1])
    selc_h = din("selc", [NE, NE * 128])

    out_h = nc.dram_tensor("out", [T, D], F32, kind="ExternalOutput").ap()

    def scrt(name, shape, dt):
        return nc.dram_tensor(name, shape, dt, kind="Internal").ap()

    sz_hbm = scrt("sz_hbm", [128, 16, T], BF16)
    y_hbm = scrt("y_hbm", [128, 16, T], BF16)
    lc_hbm = scrt("lc_hbm", [128, 16, T], BF16)
    x2_hbm = scrt("x2_hbm", [T, D], F32)
    wu_c = scrt("wu_c", [DI, DI], BF16)
    cc_in = scrt("cc_in", [128, 256], F32)
    cc_out = nc.dram_tensor("cc_out", [8, 128, 256], F32, kind="Internal",
                            addr_space="Shared").ap()

    tap = {}
    tapshape = {"xn1f": ([128, 8, T], BF16), "xcs": ([128, 16, T], BF16),
                "sz": ([128, 16, T], BF16), "y": ([128, 16, T], BF16),
                "lc": ([128, 16, T], BF16), "x2": ([T, D], F32),
                "cf": ([NE, T], BF16), "xn2f": ([128, 8, T], BF16),
                "state": ([128, 256], F32), "hin": ([128, 256], F32),
                "hq0": ([128, 16, TC], BF16), "acc0": ([128, 16, D], F32),
                "crep0": ([128, T], BF16), "cfp4": ([NE, T], BF16),
                "acc4": ([128, 16, D], F32)}
    for name in debug_taps:
        shape, dt = tapshape[name]
        tap[name] = nc.dram_tensor("tap_" + name, shape, dt,
                                   kind="ExternalOutput").ap()

    with ExitStack() as big:
        E = big.enter_context
        sem_names = ["sT", "sA", "sV", "sG", "sCN", "sCG", "sWUC", "sIPW",
                     "sP2W", "sWU", "sP3W", "sW1", "sW2", "sX0", "sX1",
                     "sSZ", "sYSP", "sLSP", "sXS0", "sXS1", "sOUT0", "sOUT1",
                     "sCC", "sP3L", "sLX0", "sLX1", "sHSL", "sTAP"]
        sems = {n: E(nc.semaphore(n)) for n in sem_names}
        PB = [E(nc.psum_tensor(f"pb{i}", [128, 512], F32)) for i in range(8)]

        BIG1 = E(nc.sbuf_tensor("BIG1", [128, 32768], BF16))
        BIG2 = E(nc.sbuf_tensor("BIG2", [128, 32768], BF16))
        BIG3 = E(nc.sbuf_tensor("BIG3", [128, 16384], BF16))
        BIG4 = E(nc.sbuf_tensor("BIG4", [128, 20480], BF16))

        def v2d(bigt, off, n, dt=BF16, p0=0, p1=128):
            ap = bigt.ap()[p0:p1, off:off + (n * 2 if dt == F32 else n)]
            if dt != BF16:
                ap = ap.bitcast(dt)
            return ap

        def v3d(bigt, off, a, b, dt=BF16):
            ap = bigt.ap()[:, off:off + (a * b * 2 if dt == F32 else a * b)]
            if dt != BF16:
                ap = ap.bitcast(dt)
            return ap.rearrange("p (a b) -> p a b", a=a)

        idbf = E(nc.sbuf_tensor("idbf", [128, 128], BF16))
        idf32 = E(nc.sbuf_tensor("idf32", [128, 128], F32))
        ones1 = E(nc.sbuf_tensor("ones1_s", [1, 128], BF16))
        selc = E(nc.sbuf_tensor("selc_s", [NE, NE * 128], BF16))
        hmask = E(nc.sbuf_tensor("hmask_s", [128, 1], F32))
        cmask = E(nc.sbuf_tensor("cmask_s", [128, 8], F32))
        ipb = E(nc.sbuf_tensor("ipb_s", [128, 32], F32))
        cwS = E(nc.sbuf_tensor("cwS", [128, 16, 4], F32))
        cbS = E(nc.sbuf_tensor("cbS", [128, 16], F32))
        xpbd = E(nc.sbuf_tensor("xpbd_s", [32, 1], F32))
        xpbdn = E(nc.sbuf_tensor("xpbdn_s", [32, 1], F32))
        signv = E(nc.sbuf_tensor("signv_s", [32, 1], F32))
        xpbu = E(nc.sbuf_tensor("xpbu_s", [128, 16], F32))
        dtbS = E(nc.sbuf_tensor("dtbS", [128, 16], F32))
        dtbn = E(nc.sbuf_tensor("dtbn", [128, 16], F32))
        dvec = E(nc.sbuf_tensor("dvec_s", [128, 16], F32))
        carry = E(nc.sbuf_tensor("carry", [128, 16, 3], F32))
        hin = E(nc.sbuf_tensor("hin", [128, 256], F32))
        b2sb = E(nc.sbuf_tensor("b2sb", [NE, D], BF16))
        bgS = E(nc.sbuf_tensor("bgS", [NE, 1], F32))
        b1sb = E(nc.sbuf_tensor("b1sb", [128, NE, HID // 128], F32))
        stt = E(nc.sbuf_tensor("stt", [128, 8], F32))
        gsc = E(nc.sbuf_tensor("gsc", [128, 8], F32))
        gT = E(nc.sbuf_tensor("gT", [128, 16, NE], F32))

        block = E(nc.Block())
        P = Planner(nc, block, sems)
        W, DO = P.wait, P.do

        for dst, src in [(idf32, ident_h), (hmask, hmask_h), (cmask, cmask_h),
                         (ipb, ipb_h), (cwS, cw_h), (cbS, cb_h),
                         (xpbd, xpbd_h), (xpbu, xpbu_h), (dtbS, dtb_h),
                         (signv, signv_h),
                         (dvec, dvec_h), (bgS, bg_h), (b1sb, b1_h)]:
            DO("sync", lambda e, d=dst, s=src: e.dma_start(out=d[:], in_=s),
               inc=("sCN", 16))
        tok_csync = ("sCN", P.cnt["sCN"])
        for dst, src in [(idbf, ident_h), (ones1, ones1_h), (selc, selc_h),
                         (b2sb, b2_h)]:
            DO("gpsimd", lambda e, d=dst, s=src: e.dma_start(out=d[:], in_=s),
               inc=("sCG", 16))
        tok_cgp = ("sCG", P.cnt["sCG"])
        tok_wuc = DO("gpsimd", lambda e: e.dma_start(out=wu_c, in_=xpwu_h),
                     inc=("sWUC", 16))

        W("vector", tok_csync)
        DO("vector", lambda e: e.tensor_mul(xpbdn[:], xpbd[:], signv[:]))
        DO("vector", lambda e: e.tensor_scalar_mul(dtbn[:], dtbS[:], -1.0))
        tok_prep = DO("vector", lambda e: e.memset(carry[:], 0.0), inc=("sV", 1))

        xt = [v2d(BIG1, 16384, 1024, F32), v2d(BIG1, 18432, 1024, F32)]
        xnb = [v2d(BIG1, 20480, 1024), v2d(BIG1, 21504, 1024)]
        sqb = v2d(BIG1, 22528, 1024, F32)
        x2t = [v2d(BIG1, 24576, 1024, F32), v2d(BIG1, 26624, 1024, F32)]
        xn1F = v3d(BIG3, 0, 8, T)
        xn1Fh = v3d(BIG4, 0, 8, 128)
        ipw_sb = v3d(BIG2, 0, 8, 2 * DI)

        tok_ipw = DO("gpsimd", lambda e: e.dma_start(
            out=ipw_sb, in_=ipw_h.rearrange("(k p) f -> p k f", p=128)),
            inc=("sIPW", 16))

        NT0 = NTILE + 1
        ld_toks, norm_toks, tr_toks, ev_toks = [], [], [], []
        for j in range(NT0):
            if j >= 2:
                W("sync", norm_toks[j - 2])
            src = x_h[j * 128:(j + 1) * 128, :] if j < NTILE else xhalo_h
            ld_toks.append(DO("sync", lambda e, s=src, b=xt[j % 2]:
                              e.dma_start(out=b, in_=s),
                              inc=("sX0" if j % 2 == 0 else "sX1", 16)))
            b, nb = xt[j % 2], xnb[j % 2]
            W("vector", ld_toks[j])
            if j >= 2:
                W("vector", norm_toks[j - 2])
            t_rs = DO("vector", lambda e, b=b: e.reduce_sum(
                stt[:, 0:1], b, axis=AX.X), inc=("sV", 1))
            W("vector", t_rs)
            t_m = DO("vector", lambda e: e.tensor_scalar_mul(
                stt[:, 1:2], stt[:, 0:1], -1.0 / D), inc=("sV", 1))
            W("scalar", t_m)
            t_sq = DO("scalar", lambda e, b=b: e.activation(
                sqb, b, AF.Square, bias=stt[:, 1:2], accum_out=stt[:, 2:3]),
                inc=("sA", 1))
            W("vector", t_sq)
            t_v = DO("vector", lambda e: e.tensor_scalar(
                stt[:, 3:4], stt[:, 2:3], 1.0 / D, EPS, ALU.mult, ALU.add),
                inc=("sV", 1))
            W("scalar", t_v)
            t_sd = DO("scalar", lambda e: e.activation(
                stt[:, 4:5], stt[:, 3:4], AF.Sqrt), inc=("sA", 1))
            W("vector", t_sd)
            t_rc = DO("vector", lambda e: e.reciprocal(stt[:, 5:6], stt[:, 4:5]),
                      inc=("sV", 1))
            W("vector", t_rc)
            t_b2 = DO("vector", lambda e: e.tensor_mul(
                stt[:, 6:7], stt[:, 1:2], stt[:, 5:6]), inc=("sV", 1))
            W("scalar", t_b2)
            norm_toks.append(DO("scalar", lambda e, b=b, nb=nb: e.activation(
                nb, b, AF.Identity, bias=stt[:, 6:7], scale=stt[:, 5:6]),
                inc=("sA", 1)))
            W("tensor", norm_toks[j])
            if j == 0:
                W("tensor", tok_cgp)
            if j >= 2:
                W("tensor", ev_toks[j - 2])
            pbb = PB[j % 2].ap().bitcast(BF16)

            def tr8(e, nb=nb, pbb=pbb):
                r = None
                for k in range(8):
                    r = e.transpose(pbb[:, k * 128:(k + 1) * 128],
                                    nb[:, k * 128:(k + 1) * 128], idbf[:])
                return r
            tr_toks.append(DO("tensor", tr8, inc=("sT", 1)))
            W("scalar", tr_toks[j])

            def ev8(e, j=j, pbb=pbb):
                r = None
                for k in range(8):
                    dst = (xn1F[:, k, j * 128:(j + 1) * 128] if j < NTILE
                           else xn1Fh[:, k, :])
                    r = e.copy(dst, pbb[:, k * 128:(k + 1) * 128])
                return r
            ev_toks.append(DO("scalar", ev8, inc=("sA", 1)))

        tok_p0 = ev_toks[-1]
        if "xn1f" in tap:
            W("sync", tok_p0)
            DO("sync", lambda e: e.dma_start(out=tap["xn1f"], in_=xn1F),
               inc=("sTAP", 16))
        P.flush()

        xcs = v3d(BIG1, 0, 16, T)
        ext = v2d(BIG4, 1024, 515, F32)
        acc0 = v2d(BIG4, 2056, 512, F32)
        acc1 = v2d(BIG4, 3080, 512, F32)
        szb = v3d(BIG4, 4104, 16, TC)
        halb = v2d(BIG4, 12296, 128, F32)

        W("tensor", tok_ipw)
        W("tensor", tok_p0)
        W("vector", tok_prep)

        cr_toks, cr_ev = [], []
        for f in range(16):
            if f >= 2:
                W("tensor", cr_ev[f - 2])

            def mmh(e, f=f):
                r = None
                for k in range(8):
                    r = e.matmul(PB[f % 2][:, 0:128],
                                 ipw_sb[:, k, f * 128:(f + 1) * 128],
                                 xn1Fh[:, k, :], start=(k == 0), stop=(k == 7))
                return r
            t_mm = DO("tensor", mmh, inc=("sT", 1))
            W("scalar", t_mm)
            if f >= 1:
                W("scalar", cr_toks[f - 1])
            t_ev = DO("scalar", lambda e, f=f: e.activation(
                halb, PB[f % 2][:, 0:128], AF.Identity, bias=ipb[:, f:f + 1]),
                inc=("sA", 1))
            cr_ev.append(t_ev)
            W("vector", t_ev)
            cr_toks.append(DO("vector", lambda e, f=f: e.tensor_scalar(
                carry[:, f, :], halb[:, 125:128], hmask[:, 0:1], None, ALU.mult),
                inc=("sV", 1)))

        p1_ev = [cr_ev[-2], cr_ev[-1], None, None]
        conv_tok = cr_toks[-1]
        si_tok = None
        sz_sp = None
        for c in range(NCH):
            c0 = c * TC
            for f in range(32):
                bank = f % 4
                W("tensor", p1_ev[bank])

                def mmf(e, f=f, c0=c0, bank=bank):
                    r = None
                    for k in range(8):
                        r = e.matmul(PB[bank][:, 0:TC],
                                     ipw_sb[:, k, f * 128:(f + 1) * 128],
                                     xn1F[:, k, c0:c0 + TC],
                                     start=(k == 0), stop=(k == 7))
                    return r
                t_mm = DO("tensor", mmf, inc=("sT", 1))
                W("scalar", t_mm)
                if f < 16:
                    W("scalar", conv_tok)
                    t_ev = DO("scalar", lambda e, bank=bank, f=f: e.activation(
                        ext[:, 3:3 + TC], PB[bank][:, 0:TC], AF.Identity,
                        bias=ipb[:, f:f + 1]), inc=("sA", 1))
                    p1_ev[bank] = t_ev
                    W("vector", t_ev)
                    W("vector", si_tok)
                    DO("vector", lambda e, f=f: e.tensor_copy(ext[:, 0:3],
                                                              carry[:, f, :]))
                    t_c1 = DO("vector", lambda e, f=f: e.tensor_copy(
                        carry[:, f, :], ext[:, TC:TC + 3]), inc=("sV", 1))
                    W("vector", t_c1)
                    t_c2 = DO("vector", lambda e, f=f: e.tensor_scalar_mul(
                        acc0, ext[:, 0:TC], cwS[:, f, 0:1]), inc=("sV", 1))
                    W("vector", t_c2)
                    t_c3 = DO("vector", lambda e, f=f: e.scalar_tensor_tensor(
                        acc1, ext[:, 1:1 + TC], cwS[:, f, 1:2], acc0,
                        ALU.mult, ALU.add), inc=("sV", 1))
                    W("vector", t_c3)
                    t_c4 = DO("vector", lambda e, f=f: e.scalar_tensor_tensor(
                        acc0, ext[:, 2:2 + TC], cwS[:, f, 2:3], acc1,
                        ALU.mult, ALU.add), inc=("sV", 1))
                    W("vector", t_c4)
                    conv_tok = DO("vector", lambda e, f=f: e.scalar_tensor_tensor(
                        acc1, ext[:, 3:3 + TC], cwS[:, f, 3:4], acc0,
                        ALU.mult, ALU.add), inc=("sV", 1))
                    W("scalar", conv_tok)
                    si_tok = DO("scalar", lambda e, f=f, c0=c0: e.activation(
                        xcs[:, f, c0:c0 + TC], acc1, AF.Silu,
                        bias=cbS[:, f:f + 1]), inc=("sA", 1))
                else:
                    fz = f - 16
                    if fz == 0 and c >= 1:
                        W("scalar", sz_sp)
                    t_ev = DO("scalar", lambda e, bank=bank, fz=fz: e.activation(
                        szb[:, fz, :], PB[bank][:, 0:TC], AF.Silu,
                        bias=ipb[:, 16 + fz:17 + fz]), inc=("sA", 1))
                    p1_ev[bank] = t_ev
            W("sync", ("sA", P.cnt["sA"]))
            sz_sp = DO("sync", lambda e, c0=c0: e.dma_start(
                out=sz_hbm[:, :, c0:c0 + TC], in_=szb), inc=("sSZ", 16))

        if "xcs" in tap:
            W("sync", ("sA", P.cnt["sA"]))
            DO("sync", lambda e: e.dma_start(out=tap["xcs"], in_=xcs),
               inc=("sTAP", 16))
        if "sz" in tap:
            W("sync", sz_sp)
            DO("sync", lambda e: e.dma_start(out=tap["sz"], in_=sz_hbm),
               inc=("sTAP", 16))
        P.flush()
        tok_p1_a = ("sA", P.cnt["sA"])
        tok_p1_t = ("sT", P.cnt["sT"])

        w_all = v3d(BIG2, 0, 16, TC2)
        lnw = v3d(BIG2, 4096, 16, TC2)
        dab = v3d(BIG2, 8192, 16, TC2)
        vpr = v3d(BIG2, 12288, 16, TC2)
        selb = v2d(BIG2, 16384, 2048, BF16, 0, 32)
        dtw = v2d(BIG2, 18432, 2048, BF16, 0, 16)
        brep = [v2d(BIG2, 20480, TC2), v2d(BIG2, 20736, TC2)]
        dsb = v2d(BIG2, 20992, TC2, BF16, 0, 32)
        onest = v2d(BIG2, 21248, TC2)
        state = v2d(BIG2, 21504, 256, F32)
        slotb = v2d(BIG2, 22016, 256, F32)
        lcst = v2d(BIG2, 22528, 16, F32)
        wd_sb = v3d(BIG2, 22592, 16, 32)
        wu_sb = v3d(BIG2, 23232, 16, TC2)
        lcf = v2d(BIG2, 27328, TC2, F32)
        lcb = v3d(BIG2, 27840, 16, TC2)
        ball = v3d(BIG3, 0, 16, TC2)
        h_all = v3d(BIG3, 4096, 16, TC2, F32)
        y_acc = v3d(BIG3, 12288, 16, TC2)

        W("gpsimd", tok_p1_t)
        W("gpsimd", tok_p1_a)
        DO("gpsimd", lambda e: e.dma_start(out=selb, in_=selb_h),
           inc=("sP2W", 16))
        DO("gpsimd", lambda e: e.dma_start(out=dtw, in_=dtw_h),
           inc=("sP2W", 16))
        DO("gpsimd", lambda e: e.dma_start(
            out=wd_sb, in_=xpwd_h.rearrange("(k p) f -> p k f", p=128)),
            inc=("sP2W", 16))
        tok_selb = tok_dtw = tok_wd = ("sP2W", P.cnt["sP2W"])
        tok_ones2 = DO("gpsimd", lambda e: e.memset(onest, 1.0), inc=("sG", 1))
        W("vector", tok_p1_a)
        DO("vector", lambda e: e.memset(state, 0.0))
        DO("vector", lambda e: e.memset(lcst, 0.0))
        tok_stz = DO("vector", lambda e: e.memset(hin[:], 0.0), inc=("sV", 1))

        def wu_load(q, wait_tok):
            W("sync", wait_tok)
            W("sync", tok_wuc)
            return DO("sync", lambda e, q=q: e.dma_start(
                out=wu_sb, in_=wu_c[:, q * TC2:(q + 1) * TC2].rearrange(
                    "(k p) f -> p k f", p=128)), inc=("sWU", 16))
        tok_wu = wu_load(0, tok_p1_t)

        bank_free = {i: None for i in range(8)}
        dsb_ev, sel_last = None, None
        yspill, lcspill = None, None
        ball_toks = []
        for c in range(NCH2):
            c0 = c * TC2
            W("tensor", tok_wd)
            W("tensor", bank_free[4])
            if c == 0:
                W("tensor", tok_p1_a)
                W("tensor", tok_stz)

            def mmdsb(e, c0=c0):
                r = None
                for k in range(16):
                    r = e.matmul(PB[4][0:32, 0:TC2], wd_sb[:, k, :],
                                 xcs[:, k, c0:c0 + TC2],
                                 start=(k == 0), stop=(k == 15))
                return r
            t_dsb = DO("tensor", mmdsb, inc=("sT", 1))
            W("scalar", t_dsb)
            W("scalar", tok_selb)
            W("scalar", sel_last)
            dsb_ev = DO("scalar", lambda e: e.activation(
                dsb[0:32, :], PB[4][0:32, 0:TC2], AF.Identity,
                bias=xpbdn[:], scale=signv[:]), inc=("sA", 1))
            bank_free[4] = dsb_ev

            sig_toks = []
            for s in range(16):
                bk = 5 + s % 2
                W("tensor", dsb_ev)
                if s == 0:
                    W("tensor", tok_dtw)
                W("tensor", bank_free[bk])
                t_mm = DO("tensor", lambda e, s=s, bk=bk: e.matmul(
                    PB[bk][:, 0:TC2], dtw[:, s * 128:(s + 1) * 128],
                    dsb[0:16, :], start=True, stop=True), inc=("sT", 1))
                W("scalar", t_mm)
                if s == 0 and ball_toks:
                    W("scalar", ball_toks[-1])
                t_sg = DO("scalar", lambda e, s=s, bk=bk: e.activation(
                    w_all[:, s, :], PB[bk][:, 0:TC2], AF.Sigmoid,
                    bias=dtbn[:, s:s + 1], scale=-1.0), inc=("sA", 1))
                sig_toks.append(t_sg)
                bank_free[bk] = t_sg
            W("scalar", sig_toks[-1])
            t_ln = DO("scalar", lambda e: e.activation(lnw[:], w_all[:], AF.Ln),
                      inc=("sA", 1))

            vy_toks = []
            for s in range(16):
                bk = 5 + s % 2
                if s % 2 == 0:
                    W("tensor", tok_wu)
                W("tensor", bank_free[bk])

                def mmu(e, s=s, c0=c0, bk=bk):
                    r = None
                    for k in range(16):
                        r = e.matmul(PB[bk][:, 0:TC2],
                                     wu_sb[:, k, (s % 2) * 128:(s % 2) * 128 + 128],
                                     xcs[:, k, c0:c0 + TC2],
                                     start=(k == 0), stop=(k == 15))
                    return r
                t_mm = DO("tensor", mmu, inc=("sT", 1))
                if s % 2 == 1:
                    nq = (s // 2 + 1) % 8
                    if not (c == NCH2 - 1 and s == 15):
                        tok_wu = wu_load(nq, t_mm)
                W("vector", t_mm)
                W("vector", t_ln if s == 0 else None)
                if s == 0:
                    W("vector", yspill)
                    W("vector", lcspill)
                    W("vector", tok_ones2)
                DO("vector", lambda e, s=s, bk=bk: e.scalar_tensor_tensor(
                    vpr[:, s, :], PB[bk][:, 0:TC2], xpbu[:, s:s + 1],
                    lnw[:, s, :], ALU.add, ALU.mult))
                t_vy = DO("vector", lambda e, s=s, bk=bk: e.tensor_scalar(
                    y_acc[:, s, :], PB[bk][:, 0:TC2], xpbu[:, s:s + 1],
                    dvec[:, s:s + 1], ALU.add, ALU.mult), inc=("sV", 1))
                vy_toks.append(t_vy)
                bank_free[bk] = t_vy
                t_lcs = DO("vector", lambda e, s=s: e.tensor_tensor_scan(
                    lcf, onest, lnw[:, s, :], lcst[:, s:s + 1],
                    ALU.mult, ALU.add), inc=("sV", 1))
                W("vector", t_lcs)
                DO("vector", lambda e, s=s: e.tensor_copy(lcb[:, s, :], lcf))
                DO("vector", lambda e, s=s: e.tensor_copy(
                    lcst[:, s:s + 1], lcf[:, TC2 - 1:TC2]))

            for i_n, n in enumerate(range(15, -1, -1)):
                pbsel = PB[4] if i_n % 2 == 0 else PB[7]
                W("tensor", dsb_ev)
                W("tensor", bank_free[4 if i_n % 2 == 0 else 7])
                t_bs = DO("tensor", lambda e, n=n, pbsel=pbsel: e.matmul(
                    pbsel[:, 0:TC2], selb[:, n * 128:(n + 1) * 128], dsb[:],
                    start=True, stop=True), inc=("sT", 1))
                sel_last = t_bs
                W("scalar", t_bs)
                if i_n >= 2:
                    W("scalar", ball_toks[-2])
                t_br = DO("scalar", lambda e, i_n=i_n, pbsel=pbsel: e.activation(
                    brep[i_n % 2][:], pbsel[:, 0:TC2], AF.Copy), inc=("sA", 1))
                bank_free[4 if i_n % 2 == 0 else 7] = t_br
                dbuf = dab if i_n % 2 == 0 else w_all
                t_da = DO("scalar", lambda e, n=n, dbuf=dbuf: e.activation(
                    dbuf[:], lnw[:], AF.Exp, scale=float(n + 1)), inc=("sA", 1))

                W("vector", t_da)
                W("vector", t_br)
                if i_n == 0:
                    W("vector", vy_toks[-1])
                DO("vector", lambda e, i_n=i_n: e.tensor_tensor(
                    ball[:], vpr[:],
                    brep[i_n % 2].unsqueeze(1).broadcast_to([128, 16, TC2]),
                    ALU.mult))

                def scans(e, dbuf=dbuf, n=n):
                    r = None
                    for s in range(16):
                        r = e.tensor_tensor_scan(
                            h_all[:, s, :], dbuf[:, s, :], ball[:, s, :],
                            state[:, s * 16 + n:s * 16 + n + 1],
                            ALU.mult, ALU.add)
                    return r
                t_scan = DO("vector", scans, inc=("sV", 1))
                W("vector", t_scan)
                DO("vector", lambda e, n=n: e.tensor_copy(
                    state[:, n:n + 241:16], h_all[:, :, TC2 - 1]))
                ball_toks.append(DO("vector", lambda e: e.tensor_add(
                    y_acc[:], y_acc[:], h_all[:]), inc=("sV", 1)))

            W("sync", ball_toks[-1])
            yspill = DO("sync", lambda e, c0=c0: e.dma_start(
                out=y_hbm[:, :, c0:c0 + TC2], in_=y_acc), inc=("sYSP", 16))
            lcspill = DO("sync", lambda e, c0=c0: e.dma_start(
                out=lc_hbm[:, :, c0:c0 + TC2], in_=lcb), inc=("sLSP", 16))

        W("sync", ball_toks[-1])
        t_st = DO("sync", lambda e: e.dma_start(out=cc_in, in_=state),
                  inc=("sCC", 16))
        W("gpsimd", t_st)
        t_cc = DO("gpsimd", lambda e: e.collective_compute(
            "AllGather", ALU.bypass,
            replica_groups=[[0, 1, 2, 3, 4, 5, 6, 7]],
            ins=[cc_in], outs=[cc_out]), inc=("sG", 1))
        hin_tok = tok_stz
        for j in range(8):
            W("gpsimd", t_cc)
            W("gpsimd", hin_tok)
            t_sl = DO("gpsimd", lambda e, j=j: e.dma_start(
                out=slotb, in_=cc_out[j]), inc=("sHSL", 16))
            W("vector", t_sl)
            hin_tok = DO("vector", lambda e, j=j: e.scalar_tensor_tensor(
                hin[:], slotb, cmask[:, j:j + 1], hin[:], ALU.mult, ALU.add),
                inc=("sV", 1))

        if "state" in tap:
            W("sync", t_st)
            DO("sync", lambda e: e.dma_start(out=tap["state"], in_=state),
               inc=("sTAP", 16))
        if "hin" in tap:
            W("sync", hin_tok)
            DO("sync", lambda e: e.dma_start(out=tap["hin"], in_=hin[:]),
               inc=("sTAP", 16))
        if "y" in tap:
            W("sync", yspill)
            DO("sync", lambda e: e.dma_start(out=tap["y"], in_=y_hbm),
               inc=("sTAP", 16))
        if "lc" in tap:
            W("sync", lcspill)
            DO("sync", lambda e: e.dma_start(out=tap["lc"], in_=lc_hbm),
               inc=("sTAP", 16))
        P.flush()
        tok_p2_t = ("sT", P.cnt["sT"])
        tok_p2_v = ("sV", P.cnt["sV"])

        ych = v3d(BIG1, 0, 16, TC)
        lch = v3d(BIG1, 8192, 16, TC)
        ow_sb = v3d(BIG2, 0, 16, D)
        ebuf = [v3d(BIG2, 16384, 16, TC), v3d(BIG2, 24576, 16, TC)]
        szch = v3d(BIG2, 24576, 16, TC)
        gfv = v2d(BIG2, 16384, 2048, F32, 0, NE)
        cfv = v2d(BIG4, 0, 2048, BF16, 0, NE)
        gbuf = v3d(BIG4, 2304, 16, TC)
        gtmp = v3d(BIG4, 10496, 16, TC)
        obias = v2d(BIG4, 18688, 1024, BF16, 0, 1)

        W("gpsimd", tok_p2_t)
        W("gpsimd", tok_p2_v)
        DO("gpsimd", lambda e: e.dma_start(
            out=ow_sb, in_=ow_h.rearrange("(s p) f -> p s f", p=128)),
            inc=("sP3W", 16))
        DO("gpsimd", lambda e: e.dma_start(out=obias, in_=ob_h),
           inc=("sP3W", 16))
        wgv = v3d(BIG4, 19712, 8, NE)
        DO("gpsimd", lambda e: e.dma_start(
            out=wgv,
            in_=wg_h.rearrange("(k p) f -> p k f", p=128)), inc=("sP3W", 16))
        tok_ow = tok_obias = tok_wg = ("sP3W", P.cnt["sP3W"])

        mmo_last = None
        o_ev = [None, None]
        tr_p3, ev_p3 = [], []
        x2_done = {}
        x2sp_toks = {}
        e_toks, gm_toks = [], []
        t_yf = None
        for c in range(NCH):
            c0 = c * TC
            W("scalar", mmo_last)
            W("scalar", e_toks[-1] if e_toks else None)
            if c == 0:
                W("scalar", yspill)
                W("scalar", lcspill)
                W("scalar", tok_p2_t)
                W("scalar", ("sA", P.cnt["sA"]))
            DO("scalar", lambda e, c0=c0: e.dma_start(
                out=ych, in_=y_hbm[:, :, c0:c0 + TC]), inc=("sP3L", 16))
            llc = DO("scalar", lambda e, c0=c0: e.dma_start(
                out=lch, in_=lc_hbm[:, :, c0:c0 + TC]), inc=("sP3L", 16))
            ly = llc

            e_toks, gm_toks = [], []
            for i_n, n in enumerate(range(15, -1, -1)):
                W("scalar", llc)
                if c == 0 and i_n == 0:
                    W("scalar", tok_ow)
                    W("scalar", hin_tok)
                if i_n >= 2:
                    W("scalar", gm_toks[i_n - 2])
                if c > 0 and i_n < 2:
                    W("scalar", t_yf)
                t_e = DO("scalar", lambda e, n=n, i_n=i_n: e.activation(
                    ebuf[i_n % 2][:], lch[:], AF.Exp, scale=float(n + 1)),
                    inc=("sA", 1))
                e_toks.append(t_e)
                W("vector", t_e)
                if c == 0 and i_n == 0:
                    W("vector", hin_tok)

                def gmul(e, n=n, i_n=i_n):
                    dst = gbuf if i_n == 0 else gtmp
                    r = None
                    for s in range(16):
                        r = e.tensor_scalar(
                            dst[:, s, :], ebuf[i_n % 2][:, s, :],
                            hin[:, s * 16 + n:s * 16 + n + 1], None, ALU.mult)
                    return r
                if i_n == 0:
                    gm_toks.append(DO("vector", gmul, inc=("sV", 1)))
                else:
                    t_gm = DO("vector", gmul, inc=("sV", 1))
                    W("vector", t_gm)
                    gm_toks.append(DO("vector", lambda e: e.tensor_add(
                        gbuf[:], gbuf[:], gtmp[:]), inc=("sV", 1)))
            W("scalar", gm_toks[-1])
            lsz = DO("scalar", lambda e, c0=c0: e.dma_start(
                out=szch, in_=sz_hbm[:, :, c0:c0 + TC]), inc=("sP3L", 16))
            W("vector", ly)
            W("vector", lsz)
            W("vector", gm_toks[-1])
            t_ya = DO("vector", lambda e: e.tensor_add(ych[:], ych[:], gbuf[:]),
                      inc=("sV", 1))
            W("vector", t_ya)
            t_yf = DO("vector", lambda e: e.tensor_mul(ych[:], ych[:], szch[:]),
                      inc=("sV", 1))

            for tb in range(TC // 128):
                jt = c * (TC // 128) + tb
                W("tensor", t_yf)
                if jt == 0:
                    W("tensor", tok_ow)
                    W("tensor", tok_obias)
                W("tensor", o_ev[tb % 2])

                def mmo(e, tb=tb):
                    r = None
                    for half in range(2):
                        pbk = PB[4 + 2 * (tb % 2) + half]
                        for s in range(16):
                            e.matmul(pbk[:, 0:512],
                                     ych[:, s, tb * 128:(tb + 1) * 128],
                                     ow_sb[:, s, half * 512:(half + 1) * 512],
                                     start=(s == 0), stop=False)
                        r = e.matmul(pbk[:, 0:512], ones1[:],
                                     obias[:, half * 512:(half + 1) * 512],
                                     start=False, stop=True)
                    return r
                t_o = DO("tensor", mmo, inc=("sT", 1))
                mmo_last = t_o
                lx = DO("sync", lambda e, jt=jt: e.dma_start(
                    out=xt[jt % 2], in_=x_h[jt * 128:(jt + 1) * 128, :]),
                    inc=("sX0" if jt % 2 == 0 else "sX1", 16))
                W("vector", t_o)
                W("vector", lx)
                W("vector", x2sp_toks.get(jt - 2))
                W("vector", x2_done.get(jt - 2))

                def resid(e, tb=tb, jt=jt):
                    r = None
                    for half in range(2):
                        pbk = PB[4 + 2 * (tb % 2) + half]
                        r = e.tensor_add(
                            x2t[jt % 2][:, half * 512:(half + 1) * 512],
                            pbk[:, 0:512],
                            xt[jt % 2][:, half * 512:(half + 1) * 512])
                    return r
                t_x2 = DO("vector", resid, inc=("sV", 1))
                o_ev[tb % 2] = t_x2
                W("vector", t_x2)
                W("sync", t_x2)
                x2sp_toks[jt] = DO("sync", lambda e, jt=jt: e.dma_start(
                    out=x2_hbm[jt * 128:(jt + 1) * 128, :], in_=x2t[jt % 2]),
                    inc=("sXS0" if jt % 2 == 0 else "sXS1", 16))

                b = x2t[jt % 2]
                t_rs = DO("vector", lambda e, b=b: e.reduce_sum(
                    stt[:, 0:1], b, axis=AX.X), inc=("sV", 1))
                W("vector", t_rs)
                t_m = DO("vector", lambda e: e.tensor_scalar_mul(
                    stt[:, 1:2], stt[:, 0:1], -1.0 / D), inc=("sV", 1))
                W("scalar", t_m)
                t_sq = DO("scalar", lambda e, b=b: e.activation(
                    sqb, b, AF.Square, bias=stt[:, 1:2], accum_out=stt[:, 2:3]),
                    inc=("sA", 1))
                W("vector", t_sq)
                t_v = DO("vector", lambda e: e.tensor_scalar(
                    stt[:, 3:4], stt[:, 2:3], 1.0 / D, EPS, ALU.mult, ALU.add),
                    inc=("sV", 1))
                W("scalar", t_v)
                t_ln2 = DO("scalar", lambda e: e.activation(
                    stt[:, 4:5], stt[:, 3:4], AF.Ln), inc=("sA", 1))
                W("scalar", t_ln2)
                t_sd = DO("scalar", lambda e: e.activation(
                    stt[:, 5:6], stt[:, 4:5], AF.Exp, scale=-0.5), inc=("sA", 1))
                W("vector", t_sd)
                t_b2 = DO("vector", lambda e: e.tensor_mul(
                    stt[:, 6:7], stt[:, 1:2], stt[:, 5:6]), inc=("sV", 1))
                W("scalar", t_b2)
                if jt >= 2:
                    W("scalar", tr_p3[jt - 2])
                t_n = DO("scalar", lambda e, b=b, jt=jt: e.activation(
                    xnb[jt % 2], b, AF.Identity, bias=stt[:, 6:7],
                    scale=stt[:, 5:6]), inc=("sA", 1))
                x2_done[jt] = t_n

                W("tensor", t_n)
                if jt >= 2:
                    W("tensor", ev_p3[jt - 2])
                pbb = PB[tb % 2].ap().bitcast(BF16)

                def tr8b(e, jt=jt, pbb=pbb):
                    r = None
                    for k in range(8):
                        r = e.transpose(
                            pbb[:, k * 128:(k + 1) * 128],
                            xnb[jt % 2][:, k * 128:(k + 1) * 128], idbf[:])
                    return r
                tr_p3.append(DO("tensor", tr8b, inc=("sT", 1)))
                W("scalar", tr_p3[-1])
                xn2F = v3d(BIG3, 0, 8, T)

                def ev8b(e, jt=jt, pbb=pbb, xn2F=xn2F):
                    r = None
                    for k in range(8):
                        r = e.copy(xn2F[:, k, jt * 128:(jt + 1) * 128],
                                   pbb[:, k * 128:(k + 1) * 128])
                    return r
                ev_p3.append(DO("scalar", ev8b, inc=("sA", 1)))

        xn2F = v3d(BIG3, 0, 8, T)
        tok_xn2 = ev_p3[-1]

        W("tensor", tok_wg)
        W("tensor", tok_xn2)
        W("tensor", o_ev[1])
        ge_toks = []
        for qg in range(4):
            W("tensor", t_yf)
            if qg >= 1:
                W("tensor", ge_toks[qg - 1])

            def mmg(e, qg=qg):
                r = None
                for k in range(8):
                    r = e.matmul(PB[6][0:NE, 0:512], wgv[:, k, :],
                                 xn2F[:, k, qg * 512:(qg + 1) * 512],
                                 start=(k == 0), stop=(k == 7))
                return r
            t_g = DO("tensor", mmg, inc=("sT", 1))
            W("scalar", t_g)
            ge_toks.append(DO("scalar", lambda e, qg=qg: e.activation(
                gfv[:, qg * 512:(qg + 1) * 512], PB[6][0:NE, 0:512], AF.Identity,
                bias=bgS[:]), inc=("sA", 1)))
        gtr_toks = []
        for j in range(NTILE):
            W("tensor", ge_toks[j // 4])
            if j >= 1:
                W("tensor", gtr_toks[j - 1])
            t_tr = DO("tensor", lambda e, j=j: e.transpose(
                PB[7][:, 0:NE], gfv[:, j * 128:(j + 1) * 128],
                idf32[0:NE, 0:NE]), inc=("sT", 1))
            W("scalar", t_tr)
            gtr_toks.append(DO("scalar", lambda e, j=j: e.copy(
                gT[:, j, :], PB[7][:, 0:NE]), inc=("sA", 1)))
        c_toks = []
        for j in range(NTILE):
            W("vector", gtr_toks[j])
            t1 = DO("vector", lambda e, j=j: e.reduce_max(
                gsc[:, 0:1], gT[:, j, :], axis=AX.X), inc=("sV", 1))
            W("vector", t1)
            t2 = DO("vector", lambda e, j=j: e.tensor_scalar(
                sqb[:, 0:NE], gT[:, j, :], gsc[:, 0:1], None, ALU.is_equal),
                inc=("sV", 1))
            W("vector", t2)
            t3 = DO("vector", lambda e, j=j: e.scalar_tensor_tensor(
                sqb[:, 8:8 + NE], sqb[:, 0:NE], -1e30, gT[:, j, :],
                ALU.mult, ALU.add), inc=("sV", 1))
            W("vector", t3)
            t4 = DO("vector", lambda e, j=j: e.reduce_max(
                gsc[:, 1:2], sqb[:, 8:8 + NE], axis=AX.X), inc=("sV", 1))
            W("vector", t4)
            DO("vector", lambda e, j=j: e.tensor_scalar(
                sqb[:, 16:16 + NE], sqb[:, 8:8 + NE], gsc[:, 1:2], None,
                ALU.is_equal))
            t_d = DO("vector", lambda e: e.tensor_sub(
                gsc[:, 2:3], gsc[:, 0:1], gsc[:, 1:2]), inc=("sV", 1))
            W("scalar", t_d)
            t_sg = DO("scalar", lambda e: e.activation(
                gsc[:, 3:4], gsc[:, 2:3], AF.Sigmoid), inc=("sA", 1))
            W("vector", t_sg)
            DO("vector", lambda e: e.tensor_scalar(
                gsc[:, 4:5], gsc[:, 3:4], -1.0, 1.0, ALU.mult, ALU.add))
            t5 = DO("vector", lambda e, j=j: e.tensor_scalar(
                sqb[:, 24:24 + NE], sqb[:, 0:NE], gsc[:, 3:4], None, ALU.mult),
                inc=("sV", 1))
            W("vector", t5)
            c_toks.append(DO("vector", lambda e, j=j: e.scalar_tensor_tensor(
                gT[:, j, :], sqb[:, 16:16 + NE], gsc[:, 4:5],
                sqb[:, 24:24 + NE], ALU.mult, ALU.add), inc=("sV", 1)))
        cf_toks = []
        for j in range(NTILE):
            W("tensor", c_toks[j])
            if j >= 1:
                W("tensor", cf_toks[j - 1])
            t_tr = DO("tensor", lambda e, j=j: e.transpose(
                PB[7][0:NE, 0:128], gT[:, j, :], idf32[:]), inc=("sT", 1))
            W("scalar", t_tr)
            cf_toks.append(DO("scalar", lambda e, j=j: e.copy(
                cfv[:, j * 128:(j + 1) * 128], PB[7][0:NE, 0:128]),
                inc=("sA", 1)))
        tok_cf = cf_toks[-1]

        if "x2" in tap:
            W("sync", x2sp_toks[NTILE - 1])
            W("sync", x2sp_toks[NTILE - 2])
            DO("sync", lambda e: e.dma_start(out=tap["x2"], in_=x2_hbm),
               inc=("sTAP", 16))
        if "xn2f" in tap:
            W("sync", tok_xn2)
            DO("sync", lambda e: e.dma_start(out=tap["xn2f"], in_=xn2F),
               inc=("sTAP", 16))
        if "cf" in tap:
            W("sync", tok_cf)
            DO("sync", lambda e: e.dma_start(out=tap["cf"], in_=cfv),
               inc=("sTAP", 16))
        P.flush()
        tok_p3_t = ("sT", P.cnt["sT"])
        tok_p3_v = ("sV", P.cnt["sV"])

        w1sb = v3d(BIG2, 0, 8, HID)
        w2sb = v3d(BIG2, 16384, 16, D)
        accT = v3d(BIG1, 0, 16, D, F32)
        h_q = v3d(BIG4, 2048, 16, TC)
        fscr = [v2d(BIG4, 10240, 1024, F32), v2d(BIG4, 12288, 1024, F32)]
        crep = v2d(BIG4, 14336, 2048)
        x2f = [v2d(BIG4, 16384, 1024, F32), v2d(BIG4, 18432, 1024, F32)]

        W("gpsimd", tok_p3_t)
        W("gpsimd", tok_p3_v)
        W("gpsimd", ("sA", P.cnt["sA"]))

        def w1_load(e_, wt):
            W("gpsimd", wt)
            return DO("gpsimd", lambda e, e_=e_: e.dma_start(
                out=w1sb, in_=w1_h[e_].rearrange("(k p) f -> p k f", p=128)),
                inc=("sW1", 16))

        def w2_load(e_, wt):
            W("gpsimd", wt)
            return DO("gpsimd", lambda e, e_=e_: e.dma_start(
                out=w2sb, in_=w2_h[e_].rearrange("(k p) f -> p k f", p=128)),
                inc=("sW2", 16))

        tok_w1 = w1_load(0, None)
        tok_w2 = w2_load(0, None)

        w1ev = [None, None, None, None]
        w2free = [None, None]
        last_crep_ev = None
        hmul = {}
        gadd = {}
        out_sp = {}
        w2_done = None
        for e_ in range(NE):
            for qq in range(4):
                W("tensor", tok_cf)
                W("tensor", last_crep_ev)
                if qq == 0:
                    W("tensor", w2free[0])
                    W("tensor", w2free[1])
                if e_ == 0 and qq == 0:
                    W("tensor", o_ev[1])
                t_cb = DO("tensor", lambda e, e_=e_, qq=qq: e.matmul(
                    PB[7][:, 0:512], selc[:, e_ * 128:(e_ + 1) * 128],
                    cfv[:, qq * 512:(qq + 1) * 512], start=True, stop=True),
                    inc=("sT", 1))
                W("scalar", t_cb)
                W("scalar", hmul.get((e_ - 1, 15)))
                last_crep_ev = DO("scalar", lambda e, qq=qq: e.activation(
                    crep[:, qq * 512:(qq + 1) * 512], PB[7][:, 0:512], AF.Copy),
                    inc=("sA", 1))
            tok_crep = last_crep_ev
            if "cfp4" in tap and e_ == 0:
                W("sync", tok_crep)
                DO("sync", lambda e: e.dma_start(out=tap["cfp4"], in_=cfv),
                   inc=("sTAP", 16))
            if "crep0" in tap and e_ == 0:
                W("sync", tok_crep)
                DO("sync", lambda e: e.dma_start(out=tap["crep0"], in_=crep),
                   inc=("sTAP", 16))

            for q in range(NCH):
                q0 = q * TC
                for hb in range(16):
                    bank = hb % 4
                    if hb == 0:
                        W("tensor", tok_w1)
                    W("tensor", w1ev[bank])

                    def mmw1(e, hb=hb, q0=q0, bank=bank):
                        r = None
                        for k in range(8):
                            r = e.matmul(PB[bank][:, 0:TC],
                                         w1sb[:, k, hb * 128:(hb + 1) * 128],
                                         xn2F[:, k, q0:q0 + TC],
                                         start=(k == 0), stop=(k == 7))
                        return r
                    t_mm = DO("tensor", mmw1, inc=("sT", 1))
                    last_w1mm = t_mm
                    W("scalar", t_mm)
                    W("scalar", w2_done)
                    t_gl = DO("scalar", lambda e, hb=hb, e_=e_, bank=bank:
                              e.activation(h_q[:, hb, :], PB[bank][:, 0:TC],
                                           AF.Gelu, bias=b1sb[:, e_, hb:hb + 1]),
                              inc=("sA", 1))
                    w1ev[bank] = t_gl
                    W("vector", t_gl)
                    if hb == 0:
                        W("vector", tok_crep)
                    hmul[(e_, hb)] = DO(
                        "vector", lambda e, hb=hb, q0=q0: e.tensor_mul(
                            h_q[:, hb, :], h_q[:, hb, :], crep[:, q0:q0 + TC]),
                        inc=("sV", 1))
                h_done = hmul[(e_, 15)]
                if "hq0" in tap and e_ == 0 and q == 0:
                    W("sync", h_done)
                    DO("sync", lambda e: e.dma_start(out=tap["hq0"], in_=h_q),
                       inc=("sTAP", 16))
                if q == NCH - 1 and e_ < NE - 1:
                    tok_w1 = w1_load(e_ + 1, last_w1mm)

                for tb in range(TC // 128):
                    jt = q * (TC // 128) + tb
                    pbk = [PB[4 + 2 * (tb % 2)], PB[5 + 2 * (tb % 2)]]
                    W("tensor", h_done)
                    if tb == 0 and q == 0:
                        W("tensor", tok_w2)
                    W("tensor", w2free[tb % 2])

                    def mmw2(e, tb=tb, e_=e_, pbk=pbk, q=q):
                        r = None
                        for half in range(2):
                            for s in range(16):
                                r = e.matmul(pbk[half][:, 0:512],
                                             h_q[:, s, tb * 128:(tb + 1) * 128],
                                             w2sb[:, s, half * 512:(half + 1) * 512],
                                             start=(s == 0),
                                             stop=(s == 15 and e_ != 0))
                            if e_ == 0:
                                r = e.matmul(
                                    pbk[half][:, 0:512],
                                    cfv[:, q * TC + tb * 128:q * TC + tb * 128 + 128],
                                    b2sb[:, half * 512:(half + 1) * 512],
                                    start=False, stop=True)
                        return r
                    t_mm = DO("tensor", mmw2, inc=("sT", 1))
                    last_w2mm = t_mm

                    if e_ < NE - 1:
                        W("scalar", t_mm)
                        W("scalar", gadd.get(jt - 2))
                        t_fe = DO("scalar", lambda e, pbk=pbk, jt=jt: [
                            e.copy(fscr[jt % 2][:, half * 512:(half + 1) * 512],
                                   pbk[half][:, 0:512]) for half in range(2)][-1],
                            inc=("sA", 1))
                        w2free[tb % 2] = t_fe
                        W("gpsimd", t_fe)
                        if e_ == 0:
                            gadd[jt] = DO("gpsimd", lambda e, jt=jt: e.tensor_copy(
                                accT[:, jt, :], fscr[jt % 2]), inc=("sG", 1))
                        else:
                            gadd[jt] = DO("gpsimd", lambda e, jt=jt: e.tensor_add(
                                accT[:, jt, :], accT[:, jt, :], fscr[jt % 2]),
                                inc=("sG", 1))
                    else:
                        W("scalar", out_sp.get(jt - 2))
                        lx2 = DO("scalar", lambda e, jt=jt: e.dma_start(
                            out=x2f[jt % 2],
                            in_=x2_hbm[jt * 128:(jt + 1) * 128, :]),
                            inc=("sLX0" if jt % 2 == 0 else "sLX1", 16))
                        W("vector", t_mm)
                        W("vector", lx2)
                        W("vector", gadd.get(jt))

                        def fin1(e, jt=jt, pbk=pbk):
                            r = None
                            for half in range(2):
                                sl = slice(half * 512, (half + 1) * 512)
                                r = e.tensor_add(x2f[jt % 2][:, sl],
                                                 pbk[half][:, 0:512],
                                                 x2f[jt % 2][:, sl])
                            return r
                        t_f1 = DO("vector", fin1, inc=("sV", 1))
                        W("vector", t_f1)
                        t_f = DO("vector", lambda e, jt=jt: e.tensor_add(
                            x2f[jt % 2], x2f[jt % 2], accT[:, jt, :]),
                            inc=("sV", 1))
                        w2free[tb % 2] = t_f
                        W("sync", t_f)
                        out_sp[jt] = DO("sync", lambda e, jt=jt: e.dma_start(
                            out=out_h[jt * 128:(jt + 1) * 128, :],
                            in_=x2f[jt % 2]), inc=("sOUT0" if jt % 2 == 0
                                                   else "sOUT1", 16))
                w2_done = last_w2mm
                if q == NCH - 1 and e_ in (0, 4) and ("acc0" in tap or "acc4" in tap):
                    tname = "acc0" if e_ == 0 else "acc4"
                    if tname in tap:
                        W("sync", gadd.get(15))
                        DO("sync", lambda e, tname=tname: e.dma_start(
                            out=tap[tname], in_=accT), inc=("sTAP", 16))
                if q == NCH - 1 and e_ < NE - 1:
                    tok_w2 = w2_load(e_ + 1, last_w2mm)

        W("sync", ("sOUT0", P.cnt["sOUT0"]))
        W("sync", ("sOUT1", P.cnt["sOUT1"]))
        if tap:
            W("sync", ("sTAP", P.cnt["sTAP"]))
        P.flush()

    return nc


def _prep_inputs(inp):
    f32 = np.float32
    w = {}
    w["ipw"] = np.ascontiguousarray(inp["L0_attn_in_proj_w"], f32)
    w["ipb"] = np.ascontiguousarray(
        np.asarray(inp["L0_attn_in_proj_b"], f32).reshape(32, 128).T)
    w["cw"] = np.ascontiguousarray(
        np.asarray(inp["L0_attn_conv_w"], f32)[:, 0, :].reshape(16, 128, 4)
        .transpose(1, 0, 2))
    w["cb"] = np.ascontiguousarray(
        np.asarray(inp["L0_attn_conv_b"], f32).reshape(16, 128).T)
    xpw = np.asarray(inp["L0_attn_x_proj_w"], f32)
    w["xpwd"] = np.ascontiguousarray(xpw[:, 0:32])
    w["xpwu"] = np.ascontiguousarray(xpw[:, 32:])
    xpb = np.asarray(inp["L0_attn_x_proj_b"], f32)
    w["xpbd"] = np.ascontiguousarray(xpb[0:32, None])
    w["xpbu"] = np.ascontiguousarray(xpb[32:].reshape(16, 128).T)
    w["dtw"] = np.ascontiguousarray(np.asarray(inp["L0_attn_dt_proj_w"], f32))
    w["dtb"] = np.ascontiguousarray(
        np.asarray(inp["L0_attn_dt_proj_b"], f32).reshape(16, 128).T)
    w["dvec"] = np.ascontiguousarray(
        np.asarray(inp["L0_attn_D"], f32).reshape(16, 128).T)
    w["ow"] = np.ascontiguousarray(np.asarray(inp["L0_attn_out_proj_w"], f32))
    w["ob"] = np.ascontiguousarray(
        np.asarray(inp["L0_attn_out_proj_b"], f32)[None, :])
    w["wg"] = np.ascontiguousarray(np.asarray(inp["L0_ffn_Wg"], f32))
    w["bg"] = np.ascontiguousarray(np.asarray(inp["L0_ffn_bg"], f32)[:, None])
    w["w1"] = np.ascontiguousarray(np.asarray(inp["L0_ffn_W1"], f32))
    w["b1"] = np.ascontiguousarray(
        np.asarray(inp["L0_ffn_b1"], f32).reshape(NE, 16, 128).transpose(2, 0, 1))
    w["w2"] = np.ascontiguousarray(np.asarray(inp["L0_ffn_W2"], f32))
    w["b2"] = np.ascontiguousarray(np.asarray(inp["L0_ffn_b2"], f32))
    w["ident"] = np.eye(128, dtype=f32)
    w["ones1"] = np.ones((1, 128), f32)
    selb = np.zeros((32, DS * 128), f32)
    for n in range(DS):
        selb[16 + n, n * 128:(n + 1) * 128] = 1.0
    w["selb"] = selb
    selc = np.zeros((NE, NE * 128), f32)
    for e_ in range(NE):
        selc[e_, e_ * 128:(e_ + 1) * 128] = 1.0
    w["selc"] = selc
    sv = np.ones((32, 1), f32)
    sv[16:32] = -1.0
    w["signv"] = sv
    return w


_NC_CACHE = {}


def kernel(**inputs):
    x_full = np.asarray(inputs["L0_attn_x"], np.float32)
    w = _prep_inputs(inputs)

    key = tuple(sorted(DEBUG_TAPS))
    if key not in _NC_CACHE:
        _NC_CACHE[key] = build_nc(DEBUG_TAPS)
    nc = _NC_CACHE[key]

    in_maps = []
    for c in range(8):
        b, h = c // 2, c % 2
        im = dict(w)
        im["x"] = np.ascontiguousarray(x_full[b, h * T:(h + 1) * T, :])
        xhalo = np.zeros((128, D), np.float32)
        if h == 1:
            xhalo[125:128, :] = x_full[b, T - 3:T, :]
        im["xhalo"] = xhalo
        im["hmask"] = np.full((128, 1), 1.0 if h == 1 else 0.0, np.float32)
        cm = np.zeros((128, 8), np.float32)
        if h == 1:
            cm[:, c - 1] = 1.0
        im["cmask"] = cm
        in_maps.append(im)

    res = run_bass_kernel_spmd(nc, in_maps, core_ids=list(range(8)))
    out = np.empty((4, 4096, D), np.float32)
    for c in range(8):
        b, h = c // 2, c % 2
        out[b, h * T:(h + 1) * T, :] = res.results[c]["out"]
    kernel._last_results = res
    return out
